# version 22
# baseline (speedup 1.0000x reference)
import contextlib
import math
import numpy as np
import concourse.bass as bass
import concourse.mybir as mybir
from concourse.bass_utils import run_bass_kernel_spmd

F32 = mybir.dt.float32
BF16 = mybir.dt.bfloat16
F16 = mybir.dt.float16
AF = mybir.ActivationFunctionType
ALU = mybir.AluOpType

NC_ = 8
EXT = [32, 28, 24, 20, 16, 12, 8, 4]
COV = [8 - kb // 4 for kb in range(32)]
RIB_OFF = np.concatenate([[0], np.cumsum([128 * c for c in COV])]).astype(int)
RIB_W = int(RIB_OFF[-1])
H = 16
SLOPE0 = 2.0 ** (-2.0 ** (-(math.log2(H) - 3.0)))
SLOPES = [SLOPE0 ** (i + 1) for i in range(H)]
QSCALE = 1.0 / math.sqrt(128.0)
STAB = -16.0
EPS = 1e-5
SENT = -42000.0

ENGS = ("tensor", "vector", "scalar", "gpsimd", "sync")


def qb_list(j):
    return [31 - j, 24 + j, 23 - j, 16 + j, 15 - j, 8 + j, 7 - j, j]


def ceil_div(a, b):
    return (a + b - 1) // b


class Prog:

    DMA_SEMS = ("g0", "g1", "gx0", "gx1", "gu0", "gu1", "gd0", "gd1",
                "s0", "sh0", "sh1", "sy0", "sy1")

    def __init__(self):
        self.q = {e: [] for e in ENGS}
        self.sems = {}
        self.cnt = {n: 0 for n in ("pe", "dve", "act") + self.DMA_SEMS}
        self.wm = {e: {n: 0 for n in self.cnt} for e in ENGS}
        self._bulk_i = 0

    def emit(self, eng, fn):
        self.q[eng].append(fn)

    def wait(self, eng, name, val):
        if val is None or val <= 0:
            return
        assert val <= self.cnt[name], f"wait on future {name}: {val} > {self.cnt[name]}"
        if self.wm[eng][name] >= val:
            return
        self.wm[eng][name] = val
        self.emit(eng, lambda e, n=name, v=val: e.wait_ge(self.sems[n], v))

    def op(self, eng, fn, inc=None, amt=1):
        if inc is None:
            self.emit(eng, fn)
            return None
        self.cnt[inc] += amt
        v = self.cnt[inc]
        self.emit(eng, lambda e, f=fn, n=inc, a=amt: f(e).then_inc(self.sems[n], a))
        return v

    def dma(self, eng, out, in_, sem, serialize=True):
        if serialize:
            self.wait(eng, sem, self.cnt[sem])
        return self.op(eng, lambda e, o=out, i=in_: e.dma_start(o, i), inc=sem, amt=16)

    BULK = ("gu0", "gu1", "gd0", "gd1")

    def bdma(self, out, in_):
        sem = self.BULK[self._bulk_i % len(self.BULK)]
        self._bulk_i += 1
        return self.dma("gpsimd", out, in_, sem)

    def bwait(self, eng):
        for s in self.BULK:
            self.dwait(eng, s)

    def dwait(self, eng, sem):
        self.wait(eng, sem, self.cnt[sem])


def build_program(trunc=99):
    nc = bass.Bass("TRN2", target_bir_lowering=False)
    P = Prog()

    d_xT4 = nc.dram_tensor("xT4", [8, 128, 16, 512], F32, kind="ExternalInput").ap()
    d_xqT = nc.dram_tensor("xqT", [128, 16, 1024], F32, kind="ExternalInput").ap()
    d_rib = nc.dram_tensor("rib", [128, RIB_W], F16, kind="ExternalInput").ap()
    d_wq = nc.dram_tensor("wq", [2048, 2048], F32, kind="ExternalInput").ap()
    d_wkv = nc.dram_tensor("wkv", [2048, 1024], F32, kind="ExternalInput").ap()
    d_wo = nc.dram_tensor("wo", [2048, 2048], F32, kind="ExternalInput").ap()
    d_upt = nc.dram_tensor("upt", [64, 128, 2048], F32, kind="ExternalInput").ap()
    d_dwt = nc.dram_tensor("dwt", [8192, 2048], F32, kind="ExternalInput").ap()
    d_boT = nc.dram_tensor("boT", [128, 16], F32, kind="ExternalInput").ap()
    d_usT = nc.dram_tensor("usT", [128, 64], F32, kind="ExternalInput").ap()
    d_dsT = nc.dram_tensor("dsT", [128, 16], F32, kind="ExternalInput").ap()
    d_osR = nc.dram_tensor("osR", [128, 1], F32, kind="ExternalInput").ap()
    d_ones = nc.dram_tensor("onesd", [128, 1], F32, kind="ExternalInput").ap()
    d_ones1 = nc.dram_tensor("ones1d", [1, 128], F32, kind="ExternalInput").ap()
    d_ident = nc.dram_tensor("identd", [128, 128], F32, kind="ExternalInput").ap()
    d_cvec = nc.dram_tensor("cvec", [128, 8], F32, kind="ExternalInput").ap()
    d_y = nc.dram_tensor("y", [1024, 2048], F32, kind="ExternalOutput").ap()
    d_h2 = nc.dram_tensor("h2scratch", [64, 128, 512], F16, kind="Internal").ap()

    with contextlib.ExitStack() as stack:
        en = stack.enter_context
        kA = en(nc.sbuf_tensor("kA", [128, 16384], F16))
        vA = en(nc.sbuf_tensor("vA", [128, 16384], BF16))
        qA = en(nc.sbuf_tensor("qA", [128, 16384], F16))
        aA = en(nc.sbuf_tensor("aA", [128, 16384], F16))
        ribA = en(nc.sbuf_tensor("ribA", [128, RIB_W], F16))
        boT_sb = en(nc.sbuf_tensor("boT_sb", [128, 16], F32))
        usT_sb = en(nc.sbuf_tensor("usT_sb", [128, 64], F32))
        dsT_sb = en(nc.sbuf_tensor("dsT_sb", [128, 16], F32))
        osR_sb = en(nc.sbuf_tensor("osR_sb", [128, 1], F32))
        dsos_sb = en(nc.sbuf_tensor("dsos_sb", [128, 16], F32))
        ones_bf = en(nc.sbuf_tensor("ones_bf", [128, 1], BF16))
        ones1_sb = en(nc.sbuf_tensor("ones1_sb", [1, 128], F32))
        ident_sb = en(nc.sbuf_tensor("ident_sb", [128, 128], F32))
        ones_f16 = en(nc.sbuf_tensor("ones_f16", [128, 1], F16))
        cv = en(nc.sbuf_tensor("cv", [128, 8], F32))
        bc_m = en(nc.sbuf_tensor("bc_m", [128, 512], F32))
        bc_r = en(nc.sbuf_tensor("bc_r", [128, 512], F32))
        pool = en(nc.psum_tensor("pool", [128, 4096], F32))
        sem_pe = en(nc.semaphore("sem_pe"))
        sem_dve = en(nc.semaphore("sem_dve"))
        sem_act = en(nc.semaphore("sem_act"))
        dma_sems = {n: en(nc.semaphore("sem_" + n)) for n in Prog.DMA_SEMS}
        P.sems = {"pe": sem_pe, "dve": sem_dve, "act": sem_act, **dma_sems}

        def bank(i, w=512):
            return pool[:, i * 512: i * 512 + w]

        y1f16 = vA[:, :].bitcast(F16)
        xqf16 = ribA[:, 0:16384]
        xq_bfv = ribA[:, 0:16384]
        r2v = kA[:, :].bitcast(F32)
        aAf = aA[:, :]
        qAf = qA[:, :]

        for out, src in ((boT_sb, d_boT), (usT_sb, d_usT), (dsT_sb, d_dsT),
                         (osR_sb, d_osR), (ones1_sb, d_ones1),
                         (ident_sb, d_ident), (cv, d_cvec)):
            P.dma("sync", out[:, :], src, "s0")
        P.dma("gpsimd", ones_bf[:, :], d_ones, "g0")
        P.dma("gpsimd", ones_f16[:, :], d_ones, "g0")
        P.dwait("vector", "s0")
        P.dwait("vector", "g0")
        dsos_done = P.op("vector", lambda e: e.tensor_scalar_mul(
            dsos_sb[:, :], dsT_sb[:, :], osR_sb[:, 0:1]), inc="dve")

        with nc.sbuf_tensor("wkv_bf", [128, 16384], F16) as wkv_bf:
            xbf = (qA[:, 0:8192], qA[:, 8192:16384])
            wkv_done = [0] * 16
            for d in range(16):
                wkv_done[d] = P.dma("gpsimd", wkv_bf[:, d * 1024:(d + 1) * 1024],
                                    d_wkv[d * 128:(d + 1) * 128, :],
                                    ("g0", "g1", "sh0", "sh1")[d % 4])
            x_done = [0] * 8
            x_done[0] = P.dma("gpsimd", xbf[0][:, :], d_xT4[0], "gx0")
            x_done[1] = P.dma("gpsimd", xbf[1][:, :], d_xT4[1], "gx1")
            pre = []
            for d in range(16):
                pre.append((xq_bfv[:, d * 1024:(d + 1) * 1024], d_xqT[:, d: d + 1, :]))
            for quarter in range(2):
                for d in range(16):
                    pre.append((aA[:, quarter * 8192 + d * 512: quarter * 8192 + (d + 1) * 512],
                                d_wq[d * 128:(d + 1) * 128, quarter * 512:(quarter + 1) * 512]))

            kv_mm_done = [0] * 8
            kv_ev_done = [0] * 8
            kv_ev_bank = [None] * 8
            for tc in range(8):
                sl = tc % 2
                P.dwait("tensor", "gx%d" % (tc % 2))
                for s_ in ("g0", "g1", "sh0", "sh1"):
                    P.dwait("tensor", s_)
                for d in range(16):
                    for g in range(4):
                        if d == 0 and tc >= 1:
                            P.wait("tensor", "dve", kv_ev_bank[tc - 1][g])
                        P.op("tensor", lambda e, d=d, g=g, sl=sl: e.matmul(
                            bank(g), wkv_bf[:, d * 1024 + g * 128: d * 1024 + (g + 1) * 128],
                            xbf[sl][:, d * 512:(d + 1) * 512],
                            start=(d == 0), stop=(d == 15)))
                    for tt in range(4):
                        last = (d == 15 and tt == 3)
                        if d == 0 and tc >= 1:
                            P.wait("tensor", "dve", kv_ev_bank[tc - 1][4 + tt])
                        P.op("tensor", lambda e, d=d, tt=tt, sl=sl: e.matmul(
                            bank(4 + tt),
                            xbf[sl][:, d * 512 + tt * 128: d * 512 + (tt + 1) * 128],
                            wkv_bf[:, d * 1024 + 512: d * 1024 + 1024],
                            start=(d == 0), stop=(d == 15)),
                            inc=("pe" if last else None))
                kv_mm_done[tc] = P.cnt["pe"]
                if tc + 2 < 8:
                    P.wait("gpsimd", "pe", kv_mm_done[tc])
                    x_done[tc + 2] = P.dma("gpsimd", xbf[sl][:, :], d_xT4[tc + 2],
                                            "gx%d" % sl, serialize=False)
                for _ in range(6):
                    if pre:
                        o_, i_ = pre.pop(0)
                        P.bdma(o_, i_)
                P.wait("vector", "pe", kv_mm_done[tc])
                evv = []
                for g in range(4):
                    evv.append(P.op("vector", lambda e, g=g, tc=tc: e.tensor_copy(
                        kA[:, g * 4096 + tc * 512: g * 4096 + (tc + 1) * 512], bank(g)),
                        inc="dve"))
                for tt in range(4):
                    evv.append(P.op("vector", lambda e, tt=tt, tc=tc: e.tensor_copy(
                        vA[:, (tc * 4 + tt) * 512: (tc * 4 + tt + 1) * 512], bank(4 + tt)),
                        inc="dve"))
                kv_ev_bank[tc] = evv
                kv_ev_done[tc] = P.cnt["dve"]

        if trunc <= 0:
            return nc_finish(nc, P)
        if True:
            xq_bf = xq_bfv
            P.wait("tensor", "dve", kv_ev_done[7])
            q_ev_done = {}
            q_mm_end = {}
            for quarter in range(4):
                qsl = quarter % 2
                if quarter >= 2:
                    P.wait("gpsimd", "pe", q_mm_end[quarter - 2])
                    for d in range(16):
                        P.bdma(aA[:, qsl * 8192 + d * 512: qsl * 8192 + (d + 1) * 512],
                               d_wq[d * 128:(d + 1) * 128,
                                    quarter * 512:(quarter + 1) * 512])
                for qf4 in range(4):
                    qf = quarter * 4 + qf4
                    for ck in range(2):
                        idx = qf * 2 + ck
                        r = idx % 2
                        if idx >= 2:
                            P.wait("tensor", "dve", q_ev_done[idx - 2])
                        P.bwait("tensor")
                        for d in range(16):
                            P.op("tensor", lambda e, d=d, qf4=qf4, ck=ck, r=r, qsl=qsl:
                                e.matmul(
                                    bank(r),
                                    aA[:, qsl * 8192 + d * 512 + qf4 * 128:
                                       qsl * 8192 + d * 512 + (qf4 + 1) * 128],
                                    xq_bf[:, d * 1024 + ck * 512: d * 1024 + (ck + 1) * 512],
                                    start=(d == 0), stop=(d == 15)),
                                inc=("pe" if d == 15 else None))
                        mmv = P.cnt["pe"]
                        P.wait("vector", "pe", mmv)
                        q_ev_done[idx] = P.op("vector", lambda e, qf=qf, ck=ck, r=r:
                            e.tensor_scalar_mul(
                                qA[:, qf * 1024 + ck * 512: qf * 1024 + (ck + 1) * 512],
                                bank(r), QSCALE), inc="dve")
                q_mm_end[quarter] = P.cnt["pe"]

        if trunc <= 1:
            return nc_finish(nc, P)
        P.wait("sync", "pe", P.cnt["pe"])
        RIBP = 2304
        rib_piece = [0] * 8
        for j_ in range(8):
            rib_piece[j_] = P.dma("sync", ribA[:, j_ * RIBP:(j_ + 1) * RIBP],
                                  d_rib[:, j_ * RIBP:(j_ + 1) * RIBP], "s0")

        def rib_need(kb):
            end = int(RIB_OFF[kb + 1])
            j_ = min(7, (end - 1) // RIBP)
            return rib_piece[j_]

        with contextlib.ExitStack() as st1:
            sS = st1.enter_context(nc.sbuf_tensor("sS", [128, 4096], F32))
            pT = st1.enter_context(nc.sbuf_tensor("pT", [128, 4096], BF16))
            linv_row = st1.enter_context(nc.sbuf_tensor("linv_row", [1, 1024], F32))
            pv_sb = st1.enter_context(nc.sbuf_tensor("pv_sb", [128, 1024], F32))
            linv_sb = st1.enter_context(nc.sbuf_tensor("linv_sb", [128, 1024], F32))
            P.wait("tensor", "dve", P.cnt["dve"])
            sc_done, pvl_done, stt_done, exp_done = {}, {}, {}, {}
            recip_done = [0] * 16
            bccp_done = [0] * 16
            attn_done = [0] * 16
            wo1_done = [0] * 16
            prev_h = [None]

            def emit_scores(h, kb):
                g = h % 4
                W = COV[kb] * 128
                sl = kb % 2
                s4 = kb % 4
                ph = prev_h[0]
                if kb >= 2:
                    P.wait("tensor", "dve", stt_done[(h, kb - 2)])
                elif ph is not None:
                    P.wait("tensor", "dve", stt_done[(ph, 30 + kb)])
                nw = ceil_div(W, 512)
                for w in range(nw):
                    wid = min(512, W - w * 512)
                    P.op("tensor", lambda e, kb=kb, g=g, sl=sl, w=w, wid=wid, h=h: e.matmul(
                        pool[:, sl * 1024 + w * 512: sl * 1024 + w * 512 + wid],
                        kA[:, g * 4096 + kb * 128: g * 4096 + (kb + 1) * 128],
                        qA[:, h * 1024 + w * 512: h * 1024 + w * 512 + wid],
                        start=True, stop=True),
                        inc=("pe" if w == nw - 1 else None))
                sc_done[(h, kb)] = P.cnt["pe"]
                P.wait("vector", "pe", sc_done[(h, kb)])
                if kb >= 4:
                    P.wait("vector", "act", exp_done[(h, kb - 4)])
                elif ph is not None:
                    P.wait("vector", "act", exp_done[(ph, 28 + kb)])
                off = int(RIB_OFF[kb])
                P.wait("vector", "s0", rib_need(kb))
                stt_done[(h, kb)] = P.op("vector", lambda e, h=h, sl=sl, s4=s4, W=W, off=off:
                    e.scalar_tensor_tensor(
                        sS[:, s4 * 1024: s4 * 1024 + W],
                        ribA[:, off: off + W], float(SLOPES[h]),
                        pool[:, sl * 1024: sl * 1024 + W],
                        ALU.mult, ALU.add), inc="dve")
                P.wait("scalar", "dve", stt_done[(h, kb)])
                if kb >= 4:
                    P.wait("scalar", "pe", pvl_done[(h, kb - 4)])
                elif ph is not None:
                    P.wait("scalar", "pe", pvl_done[(ph, 28 + kb)])
                if kb == 31:
                    prev_h[0] = h
                p4 = kb % 4
                exp_done[(h, kb)] = P.op("scalar", lambda e, s4=s4, p4=p4, W=W: e.activation(
                    pT[:, p4 * 1024: p4 * 1024 + W], sS[:, s4 * 1024: s4 * 1024 + W],
                    AF.Exp, bias=cv[:, 0:1]), inc="act")

            def emit_pvl(h, kb):
                g = h % 4
                W = COV[kb] * 128
                s4 = kb % 4
                P.wait("tensor", "act", exp_done[(h, kb)])
                ph = prev_h[0] if kb != 31 else (prev_h[0] if prev_h[0] != h else None)
                if kb == 0 and ph is not None and ph != h:
                    P.wait("tensor", "dve", pvev_done[ph])
                    P.wait("tensor", "dve", bccp_done[ph])
                nw = ceil_div(W, 512)
                for w in range(nw):
                    wid = min(512, W - w * 512)
                    stop_kb = 31 if w == 0 else 15
                    P.op("tensor", lambda e, kb=kb, g=g, s4=s4, w=w, wid=wid, sk=stop_kb: e.matmul(
                        pool[:, 2048 + w * 512: 2048 + w * 512 + wid],
                        vA[:, kb * 512 + g * 128: kb * 512 + (g + 1) * 128],
                        pT[:, s4 * 1024 + w * 512: s4 * 1024 + w * 512 + wid],
                        start=(kb == 0), stop=(kb == sk)))
                for w in range(nw):
                    wid = min(512, W - w * 512)
                    stop_kb = 31 if w == 0 else 15
                    P.op("tensor", lambda e, kb=kb, s4=s4, w=w, wid=wid, sk=stop_kb: e.matmul(
                        pool[0:1, 3072 + w * 512: 3072 + w * 512 + wid],
                        ones_bf[:, :],
                        pT[:, s4 * 1024 + w * 512: s4 * 1024 + w * 512 + wid],
                        start=(kb == 0), stop=(kb == sk)),
                        inc=("pe" if w == nw - 1 else None))
                pvl_done[(h, kb)] = P.cnt["pe"]

            HORDER = [0, 4, 8, 12, 1, 5, 9, 13, 2, 6, 10, 14, 3, 7, 11, 15]
            wo2_done = [0] * 16
            pvev_done = [0] * 16

            def emit_epi_a(h):
                P.wait("vector", "pe", pvl_done[(h, 31)])
                pvev_done[h] = P.op("vector", lambda e: e.tensor_copy(
                    pv_sb[:, :], pool[:, 2048: 3072]), inc="dve")
                P.wait("scalar", "pe", pvl_done[(h, 31)])
                P.op("scalar", lambda e: e.activation(
                    linv_row[:, :], pool[0:1, 3072: 4096], AF.Ln))
                recip_done[h] = P.op("scalar", lambda e: e.activation(
                    linv_row[:, :], linv_row[:, :], AF.Exp, scale=cv[0:1, 5:6]),
                    inc="act")

            def emit_epi_b(h, hi):
                P.wait("tensor", "act", recip_done[h])
                for w in range(2):
                    P.op("tensor", lambda e, w=w: e.matmul(
                        pool[:, 3072 + w * 512: 3072 + (w + 1) * 512], ones1_sb[:, :],
                        linv_row[:, w * 512: (w + 1) * 512], start=True, stop=True),
                        inc=("pe" if w == 1 else None))
                bc_mm = P.cnt["pe"]
                P.wait("vector", "pe", bc_mm)
                bccp_done[h] = P.op("vector", lambda e: e.tensor_copy(
                    linv_sb[:, :], pool[:, 3072: 4096]), inc="dve")
                attn_done[h] = P.op("vector", lambda e, h=h: e.tensor_mul(
                    aA[:, h * 1024: (h + 1) * 1024], pv_sb[:, :],
                    linv_sb[:, :]), inc="dve")
                P.wait("gpsimd", "pe", sc_done[(h, 31)])
                wo1_done[h] = P.dma("gpsimd", qA[:, h * 1024: (h + 1) * 1024],
                                    d_wo[h * 128: (h + 1) * 128, 0: 1024], "g1",
                                    serialize=(h == 0))
                if hi % 4 == 3:
                    g = h % 4
                    for f in range(4 * g, 4 * g + 4):
                        wo2_done[f] = P.dma("gpsimd", kA[:, f * 1024: (f + 1) * 1024],
                                            d_wo[f * 128: (f + 1) * 128, 1024: 2048], "g1")

            xq_piece_after = [0] * 8
            for j_ in range(8):
                lim = (j_ + 1) * 2048
                kbs = [kb for kb in range(32) if int(RIB_OFF[kb]) < lim]
                xq_piece_after[j_] = max(kbs)
            xq_pc = [0]

            def maybe_xq_piece(kb_done):
                while xq_pc[0] < 8 and xq_piece_after[xq_pc[0]] <= kb_done:
                    j_ = xq_pc[0]
                    P.wait("gpsimd", "dve", stt_done[(15, xq_piece_after[j_])])
                    P.dma("gpsimd", xqf16[:, j_ * 2048:(j_ + 1) * 2048],
                          d_xqT.rearrange("p d t -> p (d t)")[:, j_ * 2048:(j_ + 1) * 2048],
                          "g1", serialize=False)
                    xq_pc[0] += 1

            pend = [None]
            for hi, h in enumerate(HORDER):
                for step in range(36):
                    if step == 4 and pend[0] is not None:
                        emit_epi_b(*pend[0])
                        pend[0] = None
                    if step >= 4:
                        emit_pvl(h, step - 4)
                    if step < 32:
                        emit_scores(h, step)
                        if h == 15:
                            maybe_xq_piece(step)
                emit_epi_a(h)
                pend[0] = (h, hi)
            emit_epi_b(*pend[0])
            pend[0] = None
            maybe_xq_piece(31)
            xqf_done = P.cnt["g1"]

        if trunc <= 2:
            return nc_finish(nc, P)
        def emit_ln(srcT, dest_fn, tag, w_src_dve, w_sbf_pe, w_banks_dve, w_banks_act,
                    f16=True, apply_cb=None, w_src_half0=None):
            res = {}
            if w_src_half0 is None:
                P.wait("scalar", "dve", w_src_dve)
            P.wait("scalar", "pe", w_sbf_pe)
            P.wait("tensor", "dve", w_banks_dve)
            P.wait("tensor", "act", w_banks_act)
            with nc.sbuf_tensor(f"sbf_{tag}", [128, 2048], BF16) as sbf:
                if f16:
                    P.wait("tensor", "dve", w_src_dve)
                    for i in range(16):
                        P.op("tensor", lambda e, i=i: e.matmul(
                            pool[0:1, 2048: 2560], ones_f16[:, :],
                            srcT[:, i * 512: (i + 1) * 512],
                            start=(i == 0), stop=(i == 15)))
                mm_done = [0] * 8
                for qd in range(8):
                    if f16:
                        if qd >= 2:
                            P.wait("scalar", "pe", mm_done[qd - 2])
                        sl_ = (qd % 2) * 1024
                        sq = P.op("scalar", lambda e, qd=qd, sl_=sl_: e.square(
                            sbf[:, sl_: sl_ + 1024],
                            srcT[:, qd * 1024: (qd + 1) * 1024]), inc="act")
                        P.wait("tensor", "act", sq)
                        for i in range(2):
                            P.op("tensor", lambda e, i=i, qd=qd, sl_=sl_: e.matmul(
                                pool[0:1, 2560: 3072], ones_bf[:, :],
                                sbf[:, sl_ + i * 512: sl_ + (i + 1) * 512],
                                start=(qd == 0 and i == 0), stop=(qd == 7 and i == 1)),
                                inc=("pe" if i == 1 else None))
                        mm_done[qd] = P.cnt["pe"]
                        continue
                    if w_src_half0 is not None and qd < 4:
                        P.wait("scalar", "dve", w_src_half0)
                    else:
                        P.wait("scalar", "dve", w_src_dve)
                    if qd >= 1:
                        P.wait("scalar", "pe", mm_done[qd - 1])
                    P.op("scalar", lambda e, qd=qd: e.copy(
                        sbf[:, 0: 1024], srcT[:, qd * 1024: (qd + 1) * 1024]))
                    sq = P.op("scalar", lambda e, qd=qd: e.square(
                        sbf[:, 1024: 2048], srcT[:, qd * 1024: (qd + 1) * 1024]),
                        inc="act")
                    P.wait("tensor", "act", sq)
                    for i in range(2):
                        P.op("tensor", lambda e, i=i, qd=qd: e.matmul(
                            pool[0:1, 2048: 2560], ones_bf[:, :],
                            sbf[:, i * 512: (i + 1) * 512],
                            start=(qd == 0 and i == 0), stop=(qd == 7 and i == 1)))
                    for i in range(2):
                        P.op("tensor", lambda e, i=i, qd=qd: e.matmul(
                            pool[0:1, 2560: 3072], ones_bf[:, :],
                            sbf[:, 1024 + i * 512: 1024 + (i + 1) * 512],
                            start=(qd == 0 and i == 0), stop=(qd == 7 and i == 1)),
                            inc=("pe" if i == 1 else None))
                    mm_done[qd] = P.cnt["pe"]
                res["stats"] = mm_done[7]
            with contextlib.ExitStack() as st2:
                mrow = st2.enter_context(nc.sbuf_tensor(f"mrow_{tag}", [1, 512], F32))
                vrow = st2.enter_context(nc.sbuf_tensor(f"vrow_{tag}", [1, 512], F32))
                rrow = st2.enter_context(nc.sbuf_tensor(f"rrow_{tag}", [1, 512], F32))
                if f16:
                    bcm = bc_m[:, :].bitcast(F16)[:, 0:512]
                    bcr = bc_r[:, :].bitcast(F16)[:, 0:512]
                else:
                    bcm = bc_m[:, :]
                    bcr = bc_r[:, :]
                P.wait("scalar", "pe", mm_done[7])
                P.op("scalar", lambda e: e.mul(mrow[:, :], pool[0:1, 2048: 2560],
                                               cv[0:1, 3:4]))
                P.op("scalar", lambda e: e.mul(vrow[:, :], pool[0:1, 2560: 3072],
                                               cv[0:1, 4:5]))
                sqd = P.op("scalar", lambda e: e.square(rrow[:, :], mrow[:, :]), inc="act")
                P.wait("vector", "act", sqd)
                vd = P.op("vector", lambda e: e.tensor_sub(vrow[:, :], vrow[:, :],
                                                           rrow[:, :]), inc="dve")
                P.wait("scalar", "dve", vd)
                P.op("scalar", lambda e: e.activation(rrow[:, :], vrow[:, :], AF.Ln,
                                                      bias=cv[0:1, 1:2]))
                rid = P.op("scalar", lambda e: e.activation(rrow[:, :], rrow[:, :], AF.Exp,
                                                            scale=cv[0:1, 2:3]), inc="act")
                res["rows"] = rid
                P.wait("tensor", "act", rid)
                P.op("tensor", lambda e: e.matmul(pool[:, 3072: 3584], ones1_sb[:, :],
                                                  mrow[:, :], start=True, stop=True))
                bc = P.op("tensor", lambda e: e.matmul(pool[:, 3584: 4096], ones1_sb[:, :],
                                                       rrow[:, :], start=True, stop=True),
                          inc="pe")
                res["bc"] = bc
                P.wait("vector", "pe", bc)
                P.op("vector", lambda e: e.tensor_copy(bcm[:, :], pool[:, 3072: 3584]))
                P.op("vector", lambda e: e.tensor_copy(bcr[:, :], pool[:, 3584: 4096]),
                     inc="dve")
                last = 0
                for dt in range(16):
                    P.op("vector", lambda e, dt=dt: e.tensor_add(
                        srcT[:, dt * 512: (dt + 1) * 512],
                        srcT[:, dt * 512: (dt + 1) * 512], bcm[:, :]))
                    last = P.op("vector", lambda e, dt=dt: e.tensor_mul(
                        dest_fn(dt), srcT[:, dt * 512: (dt + 1) * 512],
                        bcr[:, :]), inc="dve")
                    if apply_cb is not None:
                        apply_cb(dt, last)
                res["last"] = last
                return res

        if trunc == 6:
            P.wait("gpsimd", "dve", attn_done[15])
            P.dma("gpsimd", d_y.rearrange("(x p) c -> p x c", p=128), aA[:, :], "g1")
            P.emit("sync", lambda e: e.wait_ge(P.sems["g1"], P.cnt["g1"]))
            return nc_finish(nc, P)
        if trunc == 7:
            P.wait("gpsimd", "pe", P.cnt["pe"])
            P.dma("gpsimd", d_y.rearrange("(x p) c -> p x c", p=128), qA[:, :], "g1")
            P.emit("sync", lambda e: e.wait_ge(P.sems["g1"], P.cnt["g1"]))
            return nc_finish(nc, P)
        with nc.sbuf_tensor("r1", [128, 16384], F16) as r1:
            P.wait("vector", "act", P.cnt["act"])
            wo_last_mm = [0]
            o_ev = {}
            for ck in range(2):
                for half in range(2):
                    arena = qA if half == 0 else kA
                    for dt8 in range(8):
                        dt = half * 8 + dt8
                        gidx = ck * 16 + half * 8 + dt8
                        r = gidx % 2
                        P.dwait("tensor", "g1")
                        P.wait("tensor", "dve", attn_done[15])
                        if gidx >= 2:
                            P.wait("tensor", "dve", o_ev[gidx - 2])
                        for f in range(16):
                            P.op("tensor", lambda e, f=f, dt8=dt8, ck=ck, r=r, arena=arena:
                                e.matmul(
                                    bank(r),
                                    arena[:, f * 1024 + dt8 * 128: f * 1024 + (dt8 + 1) * 128],
                                    aA[:, f * 1024 + ck * 512: f * 1024 + (ck + 1) * 512],
                                    start=(f == 0), stop=(f == 15)),
                                inc=("pe" if f == 15 else None))
                        mmv = P.cnt["pe"]
                        wo_last_mm[0] = mmv
                        P.wait("vector", "pe", mmv)
                        P.dwait("vector", "g1")
                        o_ev[gidx] = P.op("vector", lambda e, dt=dt, ck=ck, r=r:
                            e.scalar_tensor_tensor(
                                r1[:, ck * 8192 + dt * 512: ck * 8192 + (dt + 1) * 512],
                                xqf16[:, dt * 1024 + ck * 512: dt * 1024 + (ck + 1) * 512],
                                boT_sb[:, dt: dt + 1], bank(r), ALU.add, ALU.add),
                            inc="dve")
            if trunc == 5:
                P.wait("gpsimd", "dve", o_ev[31])
                P.dma("gpsimd", d_y.rearrange("(x p) c -> p x c", p=128), r1[:, :], "g1")
                P.emit("sync", lambda e: e.wait_ge(P.sems["g1"], P.cnt["g1"]))
                return nc_finish(nc, P)
            ln1 = [None, None]
            ln1[0] = emit_ln(
                r1[:, 0: 8192],
                lambda dt: y1f16[:, dt * 1024: dt * 1024 + 512],
                "ln1_0", w_src_dve=o_ev[15], w_sbf_pe=pvl_done[(15, 31)],
                w_banks_dve=attn_done[15], w_banks_act=0)
            ln1[1] = emit_ln(
                r1[:, 8192: 16384],
                lambda dt: y1f16[:, dt * 1024 + 512: dt * 1024 + 1024],
                "ln1_1", w_src_dve=o_ev[31], w_sbf_pe=ln1[0]["bc"],
                w_banks_dve=ln1[0]["last"], w_banks_act=ln1[0]["rows"])
            y1_done = [ln1[0]["last"], ln1[1]["last"]]
            if trunc == 4:
                P.wait("gpsimd", "dve", y1_done[1])
                P.dma("gpsimd", d_y.rearrange("(x p) c -> p x c", p=128), y1f16, "g1")
                P.emit("sync", lambda e: e.wait_ge(P.sems["g1"], P.cnt["g1"]))
                return nc_finish(nc, P)

        if trunc <= 3:
            return nc_finish(nc, P)
        with contextlib.ExitStack() as st3:
            upst = st3.enter_context(nc.sbuf_tensor("upst", [128, 4096], F16))
            dwst = st3.enter_context(nc.sbuf_tensor("dwst", [128, 6144], F16))
            h2st = st3.enter_context(nc.sbuf_tensor("h2st", [128, 1024], F16))
            y2t = st3.enter_context(nc.sbuf_tensor("y2t", [128, 2048], F32))
            def hid(ht):
                if ht < 32:
                    return aAf[:, ht * 512: (ht + 1) * 512]
                return qAf[:, (ht - 32) * 512: (ht - 31) * 512]

            up_dma = [0] * 64
            up_mm = [0] * 64
            up_ev = {}
            h2_wr = [0] * 64
            P.wait("gpsimd", "pe", wo_last_mm[0])
            P.wait("gpsimd", "dve", max(y1_done))
            P.wait("gpsimd", "act", P.cnt["act"])
            P.wait("tensor", "dve", P.cnt["dve"])
            up_dma[0] = P.dma("gpsimd", upst[:, 0: 2048], d_upt[0], "gu0")
            up_dma[1] = P.dma("gpsimd", upst[:, 2048: 4096], d_upt[1], "gu1")
            up_sgn = {}

            def emit_up_sign(ht):
                sl = ht % 2
                P.dwait("scalar", "gu%d" % sl)
                if ht >= 2:
                    P.wait("scalar", "pe", up_mm[ht - 2])
                up_sgn[ht] = P.op("scalar", lambda e, sl=sl: e.sign(
                    upst[:, sl * 2048: (sl + 1) * 2048],
                    upst[:, sl * 2048: (sl + 1) * 2048]), inc="act")

            emit_up_sign(0)
            for ht in range(64):
                sl = ht % 2
                if ht + 1 < 64:
                    emit_up_sign(ht + 1)
                sgn = up_sgn[ht]
                for ck in range(2):
                    r = ck
                    P.wait("tensor", "act", sgn)
                    P.wait("tensor", "dve", y1_done[ck])
                    if ht >= 1:
                        P.wait("tensor", "act", up_ev[(ht - 1, ck)])
                    for d in range(16):
                        P.op("tensor", lambda e, d=d, sl=sl, ck=ck, r=r: e.matmul(
                            bank(r),
                            upst[:, sl * 2048 + d * 128: sl * 2048 + (d + 1) * 128],
                            y1f16[:, d * 1024 + ck * 512: d * 1024 + (ck + 1) * 512],
                            start=(d == 0), stop=(d == 15)),
                            inc=("pe" if d == 15 else None))
                    up_mm[ht] = P.cnt["pe"]
                    P.wait("scalar", "pe", up_mm[ht])
                    if ck == 0:
                        up_ev[(ht, ck)] = P.op("scalar", lambda e, ht=ht, r=r: e.activation(
                            hid(ht), bank(r), AF.Relu, scale=usT_sb[:, ht: ht + 1]),
                            inc="act")
                    else:
                        if ht >= 2:
                            P.dwait("scalar", "sh%d" % (ht % 2))
                        up_ev[(ht, ck)] = P.op("scalar", lambda e, ht=ht, r=r: e.activation(
                            h2st[:, (ht % 2) * 512: (ht % 2 + 1) * 512], bank(r), AF.Relu,
                            scale=usT_sb[:, ht: ht + 1]), inc="act")
                        P.wait("sync", "act", up_ev[(ht, ck)])
                        h2_wr[ht] = P.dma("sync", d_h2[ht],
                                          h2st[:, (ht % 2) * 512: (ht % 2 + 1) * 512],
                                          "sh%d" % (ht % 2), serialize=(ht < 2))
                if ht + 2 < 64:
                    P.wait("gpsimd", "pe", up_mm[ht])
                    up_dma[ht + 2] = P.dma("gpsimd", upst[:, sl * 2048: (sl + 1) * 2048],
                                           d_upt[ht + 2], "gu%d" % sl, serialize=False)

            dw_mm = {}

            def emit_down(ck, h2_readback=False):
                P.wait("tensor", "dve", P.cnt["dve"])
                P.wait("tensor", "act", P.cnt["act"])
                r2_ev = [0] * 16
                for half in range(2):
                    if half == 1:
                        P.wait("tensor", "dve", r2_ev[7])
                    dw_sgn = {}

                    def emit_dw_grp(gi):
                        sl = gi % 3
                        dsem = ("gd0", "gd1", "gx0")[sl]
                        if gi >= 3:
                            P.wait("gpsimd", "pe", dw_mm[(ck, half, gi * 2 - 5)])
                        elif (ck, half) == (0, 0):
                            P.wait("gpsimd", "dve", max(y1_done))
                            P.wait("gpsimd", "pe", up_mm[63])
                        elif half == 1:
                            P.wait("gpsimd", "pe", dw_mm[(ck, 0, (61, 63, 59)[gi])])
                        else:
                            P.wait("gpsimd", "pe", dw_mm[(0, 1, (61, 63, 59)[gi])])
                        src_ap = d_dwt.rearrange("(g p) c -> p g c", p=128)[
                            :, gi * 2: gi * 2 + 2, half * 1024: (half + 1) * 1024]
                        P.dma("gpsimd", dwst[:, sl * 2048: (sl + 1) * 2048],
                              src_ap, dsem, serialize=(gi < 3))
                        P.dwait("scalar", dsem)
                        if gi >= 3:
                            P.wait("scalar", "pe", dw_mm[(ck, half, gi * 2 - 5)])
                        dw_sgn[gi] = P.op("scalar", lambda e, sl=sl: e.sign(
                            dwst[:, sl * 2048: (sl + 1) * 2048],
                            dwst[:, sl * 2048: (sl + 1) * 2048]), inc="act")

                    for gi0 in range(3):
                        emit_dw_grp(gi0)
                    for hc in range(64):
                        gi = hc // 2
                        sl = gi % 3
                        off = (sl * 2048) + (hc % 2) * 1024
                        key = (ck, half, hc)
                        P.wait("tensor", "act", dw_sgn[gi])
                        for b in range(8):
                            P.op("tensor", lambda e, b=b, off=off, hc=hc: e.matmul(
                                bank(b),
                                dwst[:, off + b * 128: off + (b + 1) * 128],
                                hid(hc), start=(hc == 0), stop=(hc == 63)),
                                inc=("pe" if b == 7 else None))
                        dw_mm[key] = P.cnt["pe"]
                        if hc % 2 == 1 and gi + 3 < 32:
                            emit_dw_grp(gi + 3)
                        if h2_readback and half == 1:
                            P.wait("sync", "pe", dw_mm[key])
                            if hc == 0:
                                P.dwait("sync", "sh0")
                                P.dwait("sync", "sh1")
                            P.dma("sync", hid(hc), d_h2[hc], "s0",
                                  serialize=(hc == 0))
                    P.wait("vector", "pe", dw_mm[(ck, half, 63)])
                    P.wait("vector", "dve", dsos_done)
                    for b in range(8):
                        dt = half * 8 + b
                        r2_ev[dt] = P.op("vector", lambda e, b=b, dt=dt, ck=ck:
                            e.scalar_tensor_tensor(
                                r2v[:, dt * 512: (dt + 1) * 512],
                                bank(b), dsos_sb[:, dt: dt + 1],
                                y1f16[:, dt * 1024 + ck * 512: dt * 1024 + (ck + 1) * 512],
                                ALU.mult, ALU.add), inc="dve")
                return r2_ev

            ln_prev = [ln1[1]]

            def emit_out(ck, r2_ev):
                tp_done = [0]

                def apply_cb(dt, val):
                    P.wait("tensor", "dve", val)
                    for tt in range(2):
                        P.op("tensor", lambda e, dt=dt, tt=tt: e.transpose(
                            pool[:, tt * 2048 + dt * 128: tt * 2048 + (dt + 1) * 128],
                            r2v[:, dt * 512 + tt * 128: dt * 512 + (tt + 1) * 128],
                            ident_sb[:, :]),
                            inc=("pe" if (dt == 15 and tt == 1) else None))
                res = emit_ln(r2v[:, :],
                              lambda dt: r2v[:, dt * 512: (dt + 1) * 512], f"ln2_{ck}",
                              w_src_dve=r2_ev[15], w_sbf_pe=ln_prev[0]["bc"],
                              w_banks_dve=max(r2_ev[15], ln_prev[0]["last"]),
                              w_banks_act=ln_prev[0]["rows"],
                              f16=False, apply_cb=apply_cb)
                ln_prev[0] = res
                st_t = {}
                ev_t = {}
                for tt in range(4):
                    grp = (tt % 2) * 2048
                    if tt >= 2:
                        P.wait("tensor", "dve", ev_t[tt - 2])
                        for dt in range(16):
                            P.op("tensor", lambda e, dt=dt, tt=tt, grp=grp: e.transpose(
                                pool[:, grp + dt * 128: grp + (dt + 1) * 128],
                                r2v[:, dt * 512 + tt * 128: dt * 512 + (tt + 1) * 128],
                                ident_sb[:, :]),
                                inc=("pe" if dt == 15 else None))
                    tmm = P.cnt["pe"]
                    P.wait("vector", "pe", tmm)
                    P.dwait("vector", "sy0")
                    P.op("vector", lambda e, grp=grp: e.tensor_copy(
                        y2t[:, 0: 1024], pool[:, grp: grp + 1024]))
                    ev_t[tt] = P.op("vector", lambda e, grp=grp: e.tensor_copy(
                        y2t[:, 1024: 2048], pool[:, grp + 1024: grp + 2048]), inc="dve")
                    P.wait("sync", "dve", ev_t[tt])
                    st_t[tt] = P.dma("sync",
                                     d_y[ck * 512 + tt * 128: ck * 512 + (tt + 1) * 128, :],
                                     y2t[:, :], "sy0", serialize=(ck == 0 and tt == 0))
                return st_t[3]

            r2e = emit_down(0, h2_readback=True)
            emit_out(0, r2e)
            P.dwait("tensor", "s0")
            r2e1 = emit_down(1)
            emit_out(1, r2e1)
            P.dwait("sync", "sy0")
            P.dwait("sync", "sy1")

    return nc_finish(nc, P)


def nc_finish(nc, P):
    with nc.Block() as block:
        @block.tensor
        def _(e):
            for fn in P.q["tensor"]:
                fn(e)

        @block.vector
        def _(e):
            for fn in P.q["vector"]:
                fn(e)

        @block.scalar
        def _(e):
            for fn in P.q["scalar"]:
                fn(e)

        @block.gpsimd
        def _(e):
            for fn in P.q["gpsimd"]:
                fn(e)

        @block.sync
        def _(e):
            for fn in P.q["sync"]:
                fn(e)

    return nc


def _prep_shared(Wq, Wkv, Wo, bo, upw, dww, us, ds, os_):
    f32 = np.float32
    wq = np.ascontiguousarray(Wq, dtype=f32)
    wkv = np.ascontiguousarray(Wkv, dtype=f32)
    wo = np.ascontiguousarray(Wo, dtype=f32)
    upt = np.ascontiguousarray(np.asarray(upw, dtype=f32).T)
    upt = np.ascontiguousarray(
        upt.reshape(16, 128, 64, 128).transpose(2, 1, 0, 3).reshape(64, 128, 2048))
    dwt = np.ascontiguousarray(np.asarray(dww, dtype=f32).T)
    boT = np.ascontiguousarray(np.asarray(bo, dtype=f32).reshape(16, 128).T)
    usT = np.ascontiguousarray(np.asarray(us, dtype=f32).reshape(64, 128).T)
    dsT = np.ascontiguousarray(np.asarray(ds, dtype=f32).reshape(16, 128).T)
    osR = np.ascontiguousarray(
        np.broadcast_to(np.asarray(os_, dtype=f32).reshape(1, 1), (128, 1)))
    return dict(wq=wq, wkv=wkv, wo=wo, upt=upt, dwt=dwt, boT=boT, usT=usT,
                dsT=dsT, osR=osR,
                cvec=np.ascontiguousarray(np.broadcast_to(np.array(
                    [[STAB, EPS, -0.5, -1.0 / 2048.0, 1.0 / 2048.0, -1.0, 0.0, 0.0]],
                    dtype=f32), (128, 8))),
                onesd=np.ones((128, 1), dtype=f32),
                ones1d=np.ones((1, 128), dtype=f32),
                identd=np.eye(128, dtype=f32))


def _make_ribbon(j):
    qbs = qb_list(j)
    rib = np.full((128, RIB_W), SENT, dtype=np.float32)
    kl = np.arange(128)[:, None]
    ql = np.arange(128)[None, :]
    for kb in range(32):
        base = int(RIB_OFF[kb])
        jg = kb * 128 + kl
        for s in range(COV[kb]):
            diff = (jg - (qbs[s] * 128 + ql)).astype(np.float32)
            rib[:, base + s * 128: base + (s + 1) * 128] = np.where(diff <= 0, diff, SENT)
    return rib.astype(np.float16)


_CACHE = {}


def kernel(L0_attn_x, L0_attn_Wq, L0_attn_Wkv, L0_attn_Wo, L0_attn_bo,
           L0_ffn_up_weight, L0_ffn_down_weight, L0_ffn_up_scale,
           L0_ffn_down_scale, L0_ffn_output_scale):
    x = np.ascontiguousarray(np.asarray(L0_attn_x), dtype=np.float32)
    shared = _prep_shared(L0_attn_Wq, L0_attn_Wkv, L0_attn_Wo, L0_attn_bo,
                          L0_ffn_up_weight, L0_ffn_down_weight, L0_ffn_up_scale,
                          L0_ffn_down_scale, L0_ffn_output_scale)

    xT4 = {}
    for seq in range(2):
        xt = np.ascontiguousarray(x[seq].T)
        xT4[seq] = np.ascontiguousarray(
            xt.reshape(16, 128, 8, 512).transpose(2, 1, 0, 3))

    in_maps = []
    ribs = {}
    for c in range(NC_):
        seq, j = c // 4, c % 4
        rows = np.concatenate([x[seq][qb * 128: (qb + 1) * 128] for qb in qb_list(j)],
                              axis=0)
        xqT = np.ascontiguousarray(
            np.ascontiguousarray(rows.T).reshape(16, 128, 1024).transpose(1, 0, 2))
        if j not in ribs:
            ribs[j] = _make_ribbon(j)
        m = dict(shared)
        m["xT4"] = xT4[seq]
        m["xqT"] = xqT
        m["rib"] = ribs[j]
        in_maps.append(m)

    if "nc" not in _CACHE:
        _CACHE["nc"] = build_program()
    nc = _CACHE["nc"]

    res = run_bass_kernel_spmd(nc, in_maps, core_ids=list(range(NC_)))

    out = np.empty((2, 4096, 2048), dtype=np.float32)
    for c in range(NC_):
        seq, j = c // 4, c % 4
        yc = np.asarray(res.results[c]["y"])
        for s, qb in enumerate(qb_list(j)):
            out[seq, qb * 128: (qb + 1) * 128, :] = yc[s * 128: (s + 1) * 128, :]
    return out


# version 23
# speedup vs baseline: 1.1717x; 1.1717x over previous
import contextlib
import math
import numpy as np
import concourse.bass as bass
import concourse.mybir as mybir
from concourse.bass_utils import run_bass_kernel_spmd

F32 = mybir.dt.float32
BF16 = mybir.dt.bfloat16
F16 = mybir.dt.float16
AF = mybir.ActivationFunctionType
ALU = mybir.AluOpType

NC_ = 8
EXT = [32, 28, 24, 20, 16, 12, 8, 4]
COV = [8 - kb // 4 for kb in range(32)]
RIB_OFF = np.concatenate([[0], np.cumsum([128 * c for c in COV])]).astype(int)
RIB_W = int(RIB_OFF[-1])
H = 16
SLOPE0 = 2.0 ** (-2.0 ** (-(math.log2(H) - 3.0)))
SLOPES = [SLOPE0 ** (i + 1) for i in range(H)]
QSCALE = 1.0 / math.sqrt(128.0)
STAB = -16.0
EPS = 1e-5
SENT = -42000.0

ENGS = ("tensor", "vector", "scalar", "gpsimd", "sync")


def qb_list(j):
    return [31 - j, 24 + j, 23 - j, 16 + j, 15 - j, 8 + j, 7 - j, j]


def ceil_div(a, b):
    return (a + b - 1) // b


class Prog:

    DMA_SEMS = ("g0", "g1", "gx0", "gx1", "gu0", "gu1", "gd0", "gd1",
                "s0", "sh0", "sh1", "sy0", "sy1")

    def __init__(self):
        self.q = {e: [] for e in ENGS}
        self.sems = {}
        self.cnt = {n: 0 for n in ("pe", "dve", "act") + self.DMA_SEMS}
        self.wm = {e: {n: 0 for n in self.cnt} for e in ENGS}
        self._bulk_i = 0

    def emit(self, eng, fn):
        self.q[eng].append(fn)

    def wait(self, eng, name, val):
        if val is None or val <= 0:
            return
        assert val <= self.cnt[name], f"wait on future {name}: {val} > {self.cnt[name]}"
        if self.wm[eng][name] >= val:
            return
        self.wm[eng][name] = val
        self.emit(eng, lambda e, n=name, v=val: e.wait_ge(self.sems[n], v))

    def op(self, eng, fn, inc=None, amt=1):
        if inc is None:
            self.emit(eng, fn)
            return None
        self.cnt[inc] += amt
        v = self.cnt[inc]
        self.emit(eng, lambda e, f=fn, n=inc, a=amt: f(e).then_inc(self.sems[n], a))
        return v

    def dma(self, eng, out, in_, sem, serialize=True):
        if serialize:
            self.wait(eng, sem, self.cnt[sem])
        return self.op(eng, lambda e, o=out, i=in_: e.dma_start(o, i), inc=sem, amt=16)

    BULK = ("gu0", "gu1", "gd0", "gd1")

    def bdma(self, out, in_):
        sem = self.BULK[self._bulk_i % len(self.BULK)]
        self._bulk_i += 1
        return self.dma("gpsimd", out, in_, sem)

    def bwait(self, eng):
        for s in self.BULK:
            self.dwait(eng, s)

    def dwait(self, eng, sem):
        self.wait(eng, sem, self.cnt[sem])


def build_program(trunc=99):
    nc = bass.Bass("TRN2", target_bir_lowering=False)
    P = Prog()

    d_xT4 = nc.dram_tensor("xT4", [8, 128, 16, 512], F32, kind="ExternalInput").ap()
    d_xqT = nc.dram_tensor("xqT", [128, 16, 1024], F32, kind="ExternalInput").ap()
    d_rib = nc.dram_tensor("rib", [128, RIB_W], F16, kind="ExternalInput").ap()
    d_wq = nc.dram_tensor("wq", [2048, 2048], F32, kind="ExternalInput").ap()
    d_wkv = nc.dram_tensor("wkv", [2048, 1024], F32, kind="ExternalInput").ap()
    d_wo = nc.dram_tensor("wo", [2048, 2048], F32, kind="ExternalInput").ap()
    d_upt = nc.dram_tensor("upt", [64, 128, 2048], F32, kind="ExternalInput").ap()
    d_dwt = nc.dram_tensor("dwt", [8192, 2048], F32, kind="ExternalInput").ap()
    d_boT = nc.dram_tensor("boT", [128, 16], F32, kind="ExternalInput").ap()
    d_usT = nc.dram_tensor("usT", [128, 64], F32, kind="ExternalInput").ap()
    d_dsT = nc.dram_tensor("dsT", [128, 16], F32, kind="ExternalInput").ap()
    d_osR = nc.dram_tensor("osR", [128, 1], F32, kind="ExternalInput").ap()
    d_ones = nc.dram_tensor("onesd", [128, 1], F32, kind="ExternalInput").ap()
    d_ones1 = nc.dram_tensor("ones1d", [1, 128], F32, kind="ExternalInput").ap()
    d_ident = nc.dram_tensor("identd", [128, 128], F32, kind="ExternalInput").ap()
    d_cvec = nc.dram_tensor("cvec", [128, 8], F32, kind="ExternalInput").ap()
    d_y = nc.dram_tensor("y", [1024, 2048], F32, kind="ExternalOutput").ap()
    d_h2 = nc.dram_tensor("h2scratch", [64, 128, 512], F16, kind="Internal").ap()

    with contextlib.ExitStack() as stack:
        en = stack.enter_context
        kA = en(nc.sbuf_tensor("kA", [128, 16384], F16))
        vA = en(nc.sbuf_tensor("vA", [128, 16384], BF16))
        qA = en(nc.sbuf_tensor("qA", [128, 16384], F16))
        aA = en(nc.sbuf_tensor("aA", [128, 16384], F16))
        ribA = en(nc.sbuf_tensor("ribA", [128, RIB_W], F16))
        boT_sb = en(nc.sbuf_tensor("boT_sb", [128, 16], F32))
        usT_sb = en(nc.sbuf_tensor("usT_sb", [128, 64], F32))
        dsT_sb = en(nc.sbuf_tensor("dsT_sb", [128, 16], F32))
        osR_sb = en(nc.sbuf_tensor("osR_sb", [128, 1], F32))
        dsos_sb = en(nc.sbuf_tensor("dsos_sb", [128, 16], F32))
        ones_bf = en(nc.sbuf_tensor("ones_bf", [128, 1], BF16))
        ones1_sb = en(nc.sbuf_tensor("ones1_sb", [1, 128], F32))
        ident_sb = en(nc.sbuf_tensor("ident_sb", [128, 128], F32))
        ones_f16 = en(nc.sbuf_tensor("ones_f16", [128, 1], F16))
        cv = en(nc.sbuf_tensor("cv", [128, 8], F32))
        bc_m = en(nc.sbuf_tensor("bc_m", [128, 512], F32))
        bc_r = en(nc.sbuf_tensor("bc_r", [128, 512], F32))
        pool = en(nc.psum_tensor("pool", [128, 4096], F32))
        sem_pe = en(nc.semaphore("sem_pe"))
        sem_dve = en(nc.semaphore("sem_dve"))
        sem_act = en(nc.semaphore("sem_act"))
        dma_sems = {n: en(nc.semaphore("sem_" + n)) for n in Prog.DMA_SEMS}
        P.sems = {"pe": sem_pe, "dve": sem_dve, "act": sem_act, **dma_sems}

        def bank(i, w=512):
            return pool[:, i * 512: i * 512 + w]

        y1f16 = vA[:, :].bitcast(F16)
        xqf16 = ribA[:, 0:16384]
        xq_bfv = ribA[:, 0:16384]
        r2v = kA[:, :].bitcast(F32)
        aAf = aA[:, :]
        qAf = qA[:, :]

        for out, src in ((boT_sb, d_boT), (usT_sb, d_usT), (dsT_sb, d_dsT),
                         (osR_sb, d_osR), (ones1_sb, d_ones1),
                         (ident_sb, d_ident), (cv, d_cvec)):
            P.dma("sync", out[:, :], src, "s0")
        P.dma("gpsimd", ones_bf[:, :], d_ones, "g0")
        P.dma("gpsimd", ones_f16[:, :], d_ones, "g0")
        P.dwait("vector", "s0")
        P.dwait("vector", "g0")
        dsos_done = P.op("vector", lambda e: e.tensor_scalar_mul(
            dsos_sb[:, :], dsT_sb[:, :], osR_sb[:, 0:1]), inc="dve")

        with nc.sbuf_tensor("wkv_bf", [128, 16384], F16) as wkv_bf:
            xbf = (qA[:, 0:8192], qA[:, 8192:16384])
            wkv_done = [0] * 16
            for d in range(16):
                wkv_done[d] = P.dma("gpsimd", wkv_bf[:, d * 1024:(d + 1) * 1024],
                                    d_wkv[d * 128:(d + 1) * 128, :],
                                    ("g0", "g1", "sh0", "sh1")[d % 4])
            x_done = [0] * 8
            x_done[0] = P.dma("gpsimd", xbf[0][:, :], d_xT4[0], "gx0")
            x_done[1] = P.dma("gpsimd", xbf[1][:, :], d_xT4[1], "gx1")
            pre = []
            for d in range(16):
                pre.append((xq_bfv[:, d * 1024:(d + 1) * 1024], d_xqT[:, d: d + 1, :]))
            for quarter in range(2):
                for d in range(16):
                    pre.append((aA[:, quarter * 8192 + d * 512: quarter * 8192 + (d + 1) * 512],
                                d_wq[d * 128:(d + 1) * 128, quarter * 512:(quarter + 1) * 512]))

            kv_mm_done = [0] * 8
            kv_ev_done = [0] * 8
            kv_ev_bank = [None] * 8
            for tc in range(8):
                sl = tc % 2
                P.dwait("tensor", "gx%d" % (tc % 2))
                for s_ in ("g0", "g1", "sh0", "sh1"):
                    P.dwait("tensor", s_)
                for d in range(16):
                    for g in range(4):
                        if d == 0 and tc >= 1:
                            P.wait("tensor", "dve", kv_ev_bank[tc - 1][g])
                        P.op("tensor", lambda e, d=d, g=g, sl=sl: e.matmul(
                            bank(g), wkv_bf[:, d * 1024 + g * 128: d * 1024 + (g + 1) * 128],
                            xbf[sl][:, d * 512:(d + 1) * 512],
                            start=(d == 0), stop=(d == 15)))
                    for tt in range(4):
                        last = (d == 15 and tt == 3)
                        if d == 0 and tc >= 1:
                            P.wait("tensor", "dve", kv_ev_bank[tc - 1][4 + tt])
                        P.op("tensor", lambda e, d=d, tt=tt, sl=sl: e.matmul(
                            bank(4 + tt),
                            xbf[sl][:, d * 512 + tt * 128: d * 512 + (tt + 1) * 128],
                            wkv_bf[:, d * 1024 + 512: d * 1024 + 1024],
                            start=(d == 0), stop=(d == 15)),
                            inc=("pe" if last else None))
                kv_mm_done[tc] = P.cnt["pe"]
                if tc + 2 < 8:
                    P.wait("gpsimd", "pe", kv_mm_done[tc])
                    x_done[tc + 2] = P.dma("gpsimd", xbf[sl][:, :], d_xT4[tc + 2],
                                            "gx%d" % sl, serialize=False)
                for _ in range(6):
                    if pre:
                        o_, i_ = pre.pop(0)
                        P.bdma(o_, i_)
                P.wait("vector", "pe", kv_mm_done[tc])
                evv = []
                for g in range(4):
                    evv.append(P.op("vector", lambda e, g=g, tc=tc: e.tensor_copy(
                        kA[:, g * 4096 + tc * 512: g * 4096 + (tc + 1) * 512], bank(g)),
                        inc="dve"))
                for tt in range(4):
                    evv.append(P.op("vector", lambda e, tt=tt, tc=tc: e.tensor_copy(
                        vA[:, (tc * 4 + tt) * 512: (tc * 4 + tt + 1) * 512], bank(4 + tt)),
                        inc="dve"))
                kv_ev_bank[tc] = evv
                kv_ev_done[tc] = P.cnt["dve"]

        if trunc <= 0:
            return nc_finish(nc, P)
        if True:
            xq_bf = xq_bfv
            P.wait("tensor", "dve", kv_ev_done[7])
            q_ev_done = {}
            q_mm_end = {}
            for quarter in range(4):
                qsl = quarter % 2
                if quarter >= 2:
                    P.wait("gpsimd", "pe", q_mm_end[quarter - 2])
                    for d in range(16):
                        P.bdma(aA[:, qsl * 8192 + d * 512: qsl * 8192 + (d + 1) * 512],
                               d_wq[d * 128:(d + 1) * 128,
                                    quarter * 512:(quarter + 1) * 512])
                for qf4 in range(4):
                    qf = quarter * 4 + qf4
                    for ck in range(2):
                        idx = qf * 2 + ck
                        r = idx % 2
                        if idx >= 2:
                            P.wait("tensor", "dve", q_ev_done[idx - 2])
                        P.bwait("tensor")
                        for d in range(16):
                            P.op("tensor", lambda e, d=d, qf4=qf4, ck=ck, r=r, qsl=qsl:
                                e.matmul(
                                    bank(r),
                                    aA[:, qsl * 8192 + d * 512 + qf4 * 128:
                                       qsl * 8192 + d * 512 + (qf4 + 1) * 128],
                                    xq_bf[:, d * 1024 + ck * 512: d * 1024 + (ck + 1) * 512],
                                    start=(d == 0), stop=(d == 15)),
                                inc=("pe" if d == 15 else None))
                        mmv = P.cnt["pe"]
                        P.wait("vector", "pe", mmv)
                        q_ev_done[idx] = P.op("vector", lambda e, qf=qf, ck=ck, r=r:
                            e.tensor_scalar_mul(
                                qA[:, qf * 1024 + ck * 512: qf * 1024 + (ck + 1) * 512],
                                bank(r), QSCALE), inc="dve")
                q_mm_end[quarter] = P.cnt["pe"]

        if trunc <= 1:
            return nc_finish(nc, P)
        P.wait("sync", "pe", P.cnt["pe"])
        RIBP = 2304
        rib_piece = [0] * 8
        for j_ in range(8):
            rib_piece[j_] = P.dma("sync", ribA[:, j_ * RIBP:(j_ + 1) * RIBP],
                                  d_rib[:, j_ * RIBP:(j_ + 1) * RIBP], "s0")

        def rib_need(kb):
            end = int(RIB_OFF[kb + 1])
            j_ = min(7, (end - 1) // RIBP)
            return rib_piece[j_]

        with contextlib.ExitStack() as st1:
            sS = st1.enter_context(nc.sbuf_tensor("sS", [128, 4096], F32))
            pT = st1.enter_context(nc.sbuf_tensor("pT", [128, 4096], BF16))
            linv_row = st1.enter_context(nc.sbuf_tensor("linv_row", [1, 1024], F32))
            pv_sb = st1.enter_context(nc.sbuf_tensor("pv_sb", [128, 1024], F32))
            linv_sb = st1.enter_context(nc.sbuf_tensor("linv_sb", [128, 1024], F32))
            P.wait("tensor", "dve", P.cnt["dve"])
            sc_done, pvl_done, stt_done, exp_done = {}, {}, {}, {}
            recip_done = [0] * 16
            bccp_done = [0] * 16
            attn_done = [0] * 16
            wo1_done = [0] * 16
            prev_h = [None]

            def emit_scores(h, kb):
                g = h % 4
                W = COV[kb] * 128
                sl = kb % 2
                s4 = kb % 4
                ph = prev_h[0]
                if kb >= 2:
                    P.wait("tensor", "dve", stt_done[(h, kb - 2)])
                elif ph is not None:
                    P.wait("tensor", "dve", stt_done[(ph, 30 + kb)])
                nw = ceil_div(W, 512)
                for w in range(nw):
                    wid = min(512, W - w * 512)
                    P.op("tensor", lambda e, kb=kb, g=g, sl=sl, w=w, wid=wid, h=h: e.matmul(
                        pool[:, sl * 1024 + w * 512: sl * 1024 + w * 512 + wid],
                        kA[:, g * 4096 + kb * 128: g * 4096 + (kb + 1) * 128],
                        qA[:, h * 1024 + w * 512: h * 1024 + w * 512 + wid],
                        start=True, stop=True),
                        inc=("pe" if w == nw - 1 else None))
                sc_done[(h, kb)] = P.cnt["pe"]
                P.wait("vector", "pe", sc_done[(h, kb)])
                if kb >= 4:
                    P.wait("vector", "act", exp_done[(h, kb - 4)])
                elif ph is not None:
                    P.wait("vector", "act", exp_done[(ph, 28 + kb)])
                off = int(RIB_OFF[kb])
                P.wait("vector", "s0", rib_need(kb))
                stt_done[(h, kb)] = P.op("vector", lambda e, h=h, sl=sl, s4=s4, W=W, off=off:
                    e.scalar_tensor_tensor(
                        sS[:, s4 * 1024: s4 * 1024 + W],
                        ribA[:, off: off + W], float(SLOPES[h]),
                        pool[:, sl * 1024: sl * 1024 + W],
                        ALU.mult, ALU.add), inc="dve")
                P.wait("scalar", "dve", stt_done[(h, kb)])
                if kb >= 4:
                    P.wait("scalar", "pe", pvl_done[(h, kb - 4)])
                elif ph is not None:
                    P.wait("scalar", "pe", pvl_done[(ph, 28 + kb)])
                if kb == 31:
                    prev_h[0] = h
                p4 = kb % 4
                exp_done[(h, kb)] = P.op("scalar", lambda e, s4=s4, p4=p4, W=W: e.activation(
                    pT[:, p4 * 1024: p4 * 1024 + W], sS[:, s4 * 1024: s4 * 1024 + W],
                    AF.Exp, bias=cv[:, 0:1]), inc="act")

            def emit_pvl(h, kb):
                g = h % 4
                W = COV[kb] * 128
                s4 = kb % 4
                P.wait("tensor", "act", exp_done[(h, kb)])
                ph = prev_h[0] if kb != 31 else (prev_h[0] if prev_h[0] != h else None)
                if kb == 0 and ph is not None and ph != h:
                    P.wait("tensor", "dve", pvev_done[ph])
                    P.wait("tensor", "dve", bccp_done[ph])
                nw = ceil_div(W, 512)
                for w in range(nw):
                    wid = min(512, W - w * 512)
                    stop_kb = 31 if w == 0 else 15
                    P.op("tensor", lambda e, kb=kb, g=g, s4=s4, w=w, wid=wid, sk=stop_kb: e.matmul(
                        pool[:, 2048 + w * 512: 2048 + w * 512 + wid],
                        vA[:, kb * 512 + g * 128: kb * 512 + (g + 1) * 128],
                        pT[:, s4 * 1024 + w * 512: s4 * 1024 + w * 512 + wid],
                        start=(kb == 0), stop=(kb == sk)))
                for w in range(nw):
                    wid = min(512, W - w * 512)
                    stop_kb = 31 if w == 0 else 15
                    P.op("tensor", lambda e, kb=kb, s4=s4, w=w, wid=wid, sk=stop_kb: e.matmul(
                        pool[0:1, 3072 + w * 512: 3072 + w * 512 + wid],
                        ones_bf[:, :],
                        pT[:, s4 * 1024 + w * 512: s4 * 1024 + w * 512 + wid],
                        start=(kb == 0), stop=(kb == sk)),
                        inc=("pe" if w == nw - 1 else None))
                pvl_done[(h, kb)] = P.cnt["pe"]

            HORDER = [0, 4, 8, 12, 1, 5, 9, 13, 2, 6, 10, 14, 3, 7, 11, 15]
            wo2_done = [0] * 16
            pvev_done = [0] * 16

            def emit_epi_a(h):
                P.wait("vector", "pe", pvl_done[(h, 31)])
                pvev_done[h] = P.op("vector", lambda e: e.tensor_copy(
                    pv_sb[:, :], pool[:, 2048: 3072]), inc="dve")
                P.wait("scalar", "pe", pvl_done[(h, 31)])
                P.op("scalar", lambda e: e.activation(
                    linv_row[:, :], pool[0:1, 3072: 4096], AF.Ln))
                recip_done[h] = P.op("scalar", lambda e: e.activation(
                    linv_row[:, :], linv_row[:, :], AF.Exp, scale=cv[0:1, 5:6]),
                    inc="act")

            def emit_epi_b(h, hi):
                P.wait("tensor", "act", recip_done[h])
                for w in range(2):
                    P.op("tensor", lambda e, w=w: e.matmul(
                        pool[:, 3072 + w * 512: 3072 + (w + 1) * 512], ones1_sb[:, :],
                        linv_row[:, w * 512: (w + 1) * 512], start=True, stop=True),
                        inc=("pe" if w == 1 else None))
                bc_mm = P.cnt["pe"]
                P.wait("vector", "pe", bc_mm)
                bccp_done[h] = P.op("vector", lambda e: e.tensor_copy(
                    linv_sb[:, :], pool[:, 3072: 4096]), inc="dve")
                attn_done[h] = P.op("vector", lambda e, h=h: e.tensor_mul(
                    aA[:, h * 1024: (h + 1) * 1024], pv_sb[:, :],
                    linv_sb[:, :]), inc="dve")
                P.wait("gpsimd", "pe", sc_done[(h, 31)])
                wo1_done[h] = P.dma("gpsimd", qA[:, h * 1024: (h + 1) * 1024],
                                    d_wo[h * 128: (h + 1) * 128, 0: 1024], "g1",
                                    serialize=(h == 0))
                if hi % 4 == 3:
                    g = h % 4
                    for f in range(4 * g, 4 * g + 4):
                        wo2_done[f] = P.dma("gpsimd", kA[:, f * 1024: (f + 1) * 1024],
                                            d_wo[f * 128: (f + 1) * 128, 1024: 2048], "g1",
                                            serialize=False)

            xq_piece_after = [0] * 8
            for j_ in range(8):
                lim = (j_ + 1) * 2048
                kbs = [kb for kb in range(32) if int(RIB_OFF[kb]) < lim]
                xq_piece_after[j_] = max(kbs)
            xq_pc = [0]

            def maybe_xq_piece(kb_done):
                while xq_pc[0] < 8 and xq_piece_after[xq_pc[0]] <= kb_done:
                    j_ = xq_pc[0]
                    P.wait("gpsimd", "dve", stt_done[(15, xq_piece_after[j_])])
                    P.dma("gpsimd", xqf16[:, j_ * 2048:(j_ + 1) * 2048],
                          d_xqT.rearrange("p d t -> p (d t)")[:, j_ * 2048:(j_ + 1) * 2048],
                          "g1", serialize=False)
                    xq_pc[0] += 1

            pend = [None]
            for hi, h in enumerate(HORDER):
                for step in range(36):
                    if step == 4 and pend[0] is not None:
                        emit_epi_b(*pend[0])
                        pend[0] = None
                    if step >= 4:
                        emit_pvl(h, step - 4)
                    if step < 32:
                        emit_scores(h, step)
                        if h == 15:
                            maybe_xq_piece(step)
                emit_epi_a(h)
                pend[0] = (h, hi)
            emit_epi_b(*pend[0])
            pend[0] = None
            maybe_xq_piece(31)
            xqf_done = P.cnt["g1"]

        if trunc <= 2:
            return nc_finish(nc, P)
        def emit_ln(srcT, dest_fn, tag, w_src_dve, w_sbf_pe, w_banks_dve, w_banks_act,
                    f16=True, apply_cb=None, w_src_half0=None):
            res = {}
            if w_src_half0 is None:
                P.wait("scalar", "dve", w_src_dve)
            P.wait("scalar", "pe", w_sbf_pe)
            P.wait("tensor", "dve", w_banks_dve)
            P.wait("tensor", "act", w_banks_act)
            with nc.sbuf_tensor(f"sbf_{tag}", [128, 2048], BF16) as sbf:
                if f16:
                    P.wait("tensor", "dve", w_src_dve)
                    for i in range(16):
                        P.op("tensor", lambda e, i=i: e.matmul(
                            pool[0:1, 2048: 2560], ones_f16[:, :],
                            srcT[:, i * 512: (i + 1) * 512],
                            start=(i == 0), stop=(i == 15)))
                mm_done = [0] * 8
                for qd in range(8):
                    if f16:
                        if qd >= 2:
                            P.wait("scalar", "pe", mm_done[qd - 2])
                        sl_ = (qd % 2) * 1024
                        sq = P.op("scalar", lambda e, qd=qd, sl_=sl_: e.square(
                            sbf[:, sl_: sl_ + 1024],
                            srcT[:, qd * 1024: (qd + 1) * 1024]), inc="act")
                        P.wait("tensor", "act", sq)
                        for i in range(2):
                            P.op("tensor", lambda e, i=i, qd=qd, sl_=sl_: e.matmul(
                                pool[0:1, 2560: 3072], ones_bf[:, :],
                                sbf[:, sl_ + i * 512: sl_ + (i + 1) * 512],
                                start=(qd == 0 and i == 0), stop=(qd == 7 and i == 1)),
                                inc=("pe" if i == 1 else None))
                        mm_done[qd] = P.cnt["pe"]
                        continue
                    if w_src_half0 is not None and qd < 4:
                        P.wait("scalar", "dve", w_src_half0)
                    else:
                        P.wait("scalar", "dve", w_src_dve)
                    if qd >= 1:
                        P.wait("scalar", "pe", mm_done[qd - 1])
                    P.op("scalar", lambda e, qd=qd: e.copy(
                        sbf[:, 0: 1024], srcT[:, qd * 1024: (qd + 1) * 1024]))
                    sq = P.op("scalar", lambda e, qd=qd: e.square(
                        sbf[:, 1024: 2048], srcT[:, qd * 1024: (qd + 1) * 1024]),
                        inc="act")
                    P.wait("tensor", "act", sq)
                    for i in range(2):
                        P.op("tensor", lambda e, i=i, qd=qd: e.matmul(
                            pool[0:1, 2048: 2560], ones_bf[:, :],
                            sbf[:, i * 512: (i + 1) * 512],
                            start=(qd == 0 and i == 0), stop=(qd == 7 and i == 1)))
                    for i in range(2):
                        P.op("tensor", lambda e, i=i, qd=qd: e.matmul(
                            pool[0:1, 2560: 3072], ones_bf[:, :],
                            sbf[:, 1024 + i * 512: 1024 + (i + 1) * 512],
                            start=(qd == 0 and i == 0), stop=(qd == 7 and i == 1)),
                            inc=("pe" if i == 1 else None))
                    mm_done[qd] = P.cnt["pe"]
                res["stats"] = mm_done[7]
            with contextlib.ExitStack() as st2:
                mrow = st2.enter_context(nc.sbuf_tensor(f"mrow_{tag}", [1, 512], F32))
                vrow = st2.enter_context(nc.sbuf_tensor(f"vrow_{tag}", [1, 512], F32))
                rrow = st2.enter_context(nc.sbuf_tensor(f"rrow_{tag}", [1, 512], F32))
                if f16:
                    bcm = bc_m[:, :].bitcast(F16)[:, 0:512]
                    bcr = bc_r[:, :].bitcast(F16)[:, 0:512]
                else:
                    bcm = bc_m[:, :]
                    bcr = bc_r[:, :]
                P.wait("scalar", "pe", mm_done[7])
                P.op("scalar", lambda e: e.mul(mrow[:, :], pool[0:1, 2048: 2560],
                                               cv[0:1, 3:4]))
                P.op("scalar", lambda e: e.mul(vrow[:, :], pool[0:1, 2560: 3072],
                                               cv[0:1, 4:5]))
                sqd = P.op("scalar", lambda e: e.square(rrow[:, :], mrow[:, :]), inc="act")
                P.wait("vector", "act", sqd)
                vd = P.op("vector", lambda e: e.tensor_sub(vrow[:, :], vrow[:, :],
                                                           rrow[:, :]), inc="dve")
                P.wait("scalar", "dve", vd)
                P.op("scalar", lambda e: e.activation(rrow[:, :], vrow[:, :], AF.Ln,
                                                      bias=cv[0:1, 1:2]))
                rid = P.op("scalar", lambda e: e.activation(rrow[:, :], rrow[:, :], AF.Exp,
                                                            scale=cv[0:1, 2:3]), inc="act")
                res["rows"] = rid
                P.wait("tensor", "act", rid)
                P.op("tensor", lambda e: e.matmul(pool[:, 3072: 3584], ones1_sb[:, :],
                                                  mrow[:, :], start=True, stop=True))
                bc = P.op("tensor", lambda e: e.matmul(pool[:, 3584: 4096], ones1_sb[:, :],
                                                       rrow[:, :], start=True, stop=True),
                          inc="pe")
                res["bc"] = bc
                P.wait("vector", "pe", bc)
                P.op("vector", lambda e: e.tensor_copy(bcm[:, :], pool[:, 3072: 3584]))
                P.op("vector", lambda e: e.tensor_copy(bcr[:, :], pool[:, 3584: 4096]),
                     inc="dve")
                last = 0
                for dt in range(16):
                    P.op("vector", lambda e, dt=dt: e.tensor_add(
                        srcT[:, dt * 512: (dt + 1) * 512],
                        srcT[:, dt * 512: (dt + 1) * 512], bcm[:, :]))
                    last = P.op("vector", lambda e, dt=dt: e.tensor_mul(
                        dest_fn(dt), srcT[:, dt * 512: (dt + 1) * 512],
                        bcr[:, :]), inc="dve")
                    if apply_cb is not None:
                        apply_cb(dt, last)
                res["last"] = last
                return res

        if trunc == 6:
            P.wait("gpsimd", "dve", attn_done[15])
            P.dma("gpsimd", d_y.rearrange("(x p) c -> p x c", p=128), aA[:, :], "g1")
            P.emit("sync", lambda e: e.wait_ge(P.sems["g1"], P.cnt["g1"]))
            return nc_finish(nc, P)
        if trunc == 7:
            P.wait("gpsimd", "pe", P.cnt["pe"])
            P.dma("gpsimd", d_y.rearrange("(x p) c -> p x c", p=128), qA[:, :], "g1")
            P.emit("sync", lambda e: e.wait_ge(P.sems["g1"], P.cnt["g1"]))
            return nc_finish(nc, P)
        with nc.sbuf_tensor("r1", [128, 16384], F16) as r1:
            P.wait("vector", "act", P.cnt["act"])
            wo_last_mm = [0]
            o_ev = {}
            for ck in range(2):
                for half in range(2):
                    arena = qA if half == 0 else kA
                    for dt8 in range(8):
                        dt = half * 8 + dt8
                        gidx = ck * 16 + half * 8 + dt8
                        r = gidx % 2
                        P.dwait("tensor", "g1")
                        P.wait("tensor", "dve", attn_done[15])
                        if gidx >= 2:
                            P.wait("tensor", "dve", o_ev[gidx - 2])
                        for f in range(16):
                            P.op("tensor", lambda e, f=f, dt8=dt8, ck=ck, r=r, arena=arena:
                                e.matmul(
                                    bank(r),
                                    arena[:, f * 1024 + dt8 * 128: f * 1024 + (dt8 + 1) * 128],
                                    aA[:, f * 1024 + ck * 512: f * 1024 + (ck + 1) * 512],
                                    start=(f == 0), stop=(f == 15)),
                                inc=("pe" if f == 15 else None))
                        mmv = P.cnt["pe"]
                        wo_last_mm[0] = mmv
                        P.wait("vector", "pe", mmv)
                        P.dwait("vector", "g1")
                        o_ev[gidx] = P.op("vector", lambda e, dt=dt, ck=ck, r=r:
                            e.scalar_tensor_tensor(
                                r1[:, ck * 8192 + dt * 512: ck * 8192 + (dt + 1) * 512],
                                xqf16[:, dt * 1024 + ck * 512: dt * 1024 + (ck + 1) * 512],
                                boT_sb[:, dt: dt + 1], bank(r), ALU.add, ALU.add),
                            inc="dve")
            if trunc == 5:
                P.wait("gpsimd", "dve", o_ev[31])
                P.dma("gpsimd", d_y.rearrange("(x p) c -> p x c", p=128), r1[:, :], "g1")
                P.emit("sync", lambda e: e.wait_ge(P.sems["g1"], P.cnt["g1"]))
                return nc_finish(nc, P)
            ln1 = [None, None]
            ln1[0] = emit_ln(
                r1[:, 0: 8192],
                lambda dt: y1f16[:, dt * 1024: dt * 1024 + 512],
                "ln1_0", w_src_dve=o_ev[15], w_sbf_pe=pvl_done[(15, 31)],
                w_banks_dve=attn_done[15], w_banks_act=0)
            ln1[1] = emit_ln(
                r1[:, 8192: 16384],
                lambda dt: y1f16[:, dt * 1024 + 512: dt * 1024 + 1024],
                "ln1_1", w_src_dve=o_ev[31], w_sbf_pe=ln1[0]["bc"],
                w_banks_dve=ln1[0]["last"], w_banks_act=ln1[0]["rows"])
            y1_done = [ln1[0]["last"], ln1[1]["last"]]
            if trunc == 4:
                P.wait("gpsimd", "dve", y1_done[1])
                P.dma("gpsimd", d_y.rearrange("(x p) c -> p x c", p=128), y1f16, "g1")
                P.emit("sync", lambda e: e.wait_ge(P.sems["g1"], P.cnt["g1"]))
                return nc_finish(nc, P)

        if trunc <= 3:
            return nc_finish(nc, P)
        with contextlib.ExitStack() as st3:
            upst = st3.enter_context(nc.sbuf_tensor("upst", [128, 4096], F16))
            dwst = st3.enter_context(nc.sbuf_tensor("dwst", [128, 6144], F16))
            h2st = st3.enter_context(nc.sbuf_tensor("h2st", [128, 1024], F16))
            y2t = st3.enter_context(nc.sbuf_tensor("y2t", [128, 2048], F32))
            def hid(ht):
                if ht < 32:
                    return aAf[:, ht * 512: (ht + 1) * 512]
                return qAf[:, (ht - 32) * 512: (ht - 31) * 512]

            up_dma = [0] * 64
            up_mm = [0] * 64
            up_ev = {}
            h2_wr = [0] * 64
            P.wait("gpsimd", "pe", wo_last_mm[0])
            P.wait("gpsimd", "dve", y1_done[0])
            P.wait("gpsimd", "act", ln1[0]["rows"])
            P.wait("tensor", "dve", o_ev[31])
            up_dma[0] = P.dma("gpsimd", upst[:, 0: 2048], d_upt[0], "gu0")
            up_dma[1] = P.dma("gpsimd", upst[:, 2048: 4096], d_upt[1], "gu1")
            up_sgn = {}

            def emit_up_sign(ht):
                sl = ht % 2
                P.dwait("scalar", "gu%d" % sl)
                if ht >= 2:
                    P.wait("scalar", "pe", up_mm[ht - 2])
                up_sgn[ht] = P.op("scalar", lambda e, sl=sl: e.sign(
                    upst[:, sl * 2048: (sl + 1) * 2048],
                    upst[:, sl * 2048: (sl + 1) * 2048]), inc="act")

            emit_up_sign(0)
            for ht in range(64):
                sl = ht % 2
                if ht + 1 < 64:
                    emit_up_sign(ht + 1)
                sgn = up_sgn[ht]
                for ck in range(2):
                    r = ck
                    P.wait("tensor", "act", sgn)
                    P.wait("tensor", "dve", y1_done[ck])
                    if ht >= 1:
                        P.wait("tensor", "act", up_ev[(ht - 1, ck)])
                    for d in range(16):
                        P.op("tensor", lambda e, d=d, sl=sl, ck=ck, r=r: e.matmul(
                            bank(r),
                            upst[:, sl * 2048 + d * 128: sl * 2048 + (d + 1) * 128],
                            y1f16[:, d * 1024 + ck * 512: d * 1024 + (ck + 1) * 512],
                            start=(d == 0), stop=(d == 15)),
                            inc=("pe" if d == 15 else None))
                    up_mm[ht] = P.cnt["pe"]
                    P.wait("scalar", "pe", up_mm[ht])
                    if ck == 0:
                        up_ev[(ht, ck)] = P.op("scalar", lambda e, ht=ht, r=r: e.activation(
                            hid(ht), bank(r), AF.Relu, scale=usT_sb[:, ht: ht + 1]),
                            inc="act")
                    else:
                        if ht >= 2:
                            P.dwait("scalar", "sh%d" % (ht % 2))
                        up_ev[(ht, ck)] = P.op("scalar", lambda e, ht=ht, r=r: e.activation(
                            h2st[:, (ht % 2) * 512: (ht % 2 + 1) * 512], bank(r), AF.Relu,
                            scale=usT_sb[:, ht: ht + 1]), inc="act")
                        P.wait("sync", "act", up_ev[(ht, ck)])
                        h2_wr[ht] = P.dma("sync", d_h2[ht],
                                          h2st[:, (ht % 2) * 512: (ht % 2 + 1) * 512],
                                          "sh%d" % (ht % 2), serialize=(ht < 2))
                if ht + 2 < 64:
                    P.wait("gpsimd", "pe", up_mm[ht])
                    up_dma[ht + 2] = P.dma("gpsimd", upst[:, sl * 2048: (sl + 1) * 2048],
                                           d_upt[ht + 2], "gu%d" % sl, serialize=False)

            dw_mm = {}

            def emit_down(ck, h2_readback=False):
                P.wait("tensor", "dve", P.cnt["dve"])
                P.wait("tensor", "act", P.cnt["act"])
                r2_ev = [0] * 16
                for half in range(2):
                    if half == 1:
                        P.wait("tensor", "dve", r2_ev[7])
                    dw_sgn = {}

                    def emit_dw_grp(gi):
                        sl = gi % 3
                        dsem = ("gd0", "gd1", "gx0")[sl]
                        if gi >= 3:
                            P.wait("gpsimd", "pe", dw_mm[(ck, half, gi * 2 - 5)])
                        elif (ck, half) == (0, 0):
                            P.wait("gpsimd", "dve", max(y1_done))
                            P.wait("gpsimd", "pe", up_mm[63])
                        elif half == 1:
                            P.wait("gpsimd", "pe", dw_mm[(ck, 0, (61, 63, 59)[gi])])
                        else:
                            P.wait("gpsimd", "pe", dw_mm[(0, 1, (61, 63, 59)[gi])])
                        src_ap = d_dwt.rearrange("(g p) c -> p g c", p=128)[
                            :, gi * 2: gi * 2 + 2, half * 1024: (half + 1) * 1024]
                        P.dma("gpsimd", dwst[:, sl * 2048: (sl + 1) * 2048],
                              src_ap, dsem, serialize=(gi < 3))
                        P.dwait("scalar", dsem)
                        if gi >= 3:
                            P.wait("scalar", "pe", dw_mm[(ck, half, gi * 2 - 5)])
                        dw_sgn[gi] = P.op("scalar", lambda e, sl=sl: e.sign(
                            dwst[:, sl * 2048: (sl + 1) * 2048],
                            dwst[:, sl * 2048: (sl + 1) * 2048]), inc="act")

                    for gi0 in range(3):
                        emit_dw_grp(gi0)
                    for hc in range(64):
                        gi = hc // 2
                        sl = gi % 3
                        off = (sl * 2048) + (hc % 2) * 1024
                        key = (ck, half, hc)
                        P.wait("tensor", "act", dw_sgn[gi])
                        for b in range(8):
                            P.op("tensor", lambda e, b=b, off=off, hc=hc: e.matmul(
                                bank(b),
                                dwst[:, off + b * 128: off + (b + 1) * 128],
                                hid(hc), start=(hc == 0), stop=(hc == 63)),
                                inc=("pe" if b == 7 else None))
                        dw_mm[key] = P.cnt["pe"]
                        if hc % 2 == 1 and gi + 3 < 32:
                            emit_dw_grp(gi + 3)
                        if h2_readback and half == 1:
                            P.wait("sync", "pe", dw_mm[key])
                            if hc == 0:
                                P.dwait("sync", "sh0")
                                P.dwait("sync", "sh1")
                            P.dma("sync", hid(hc), d_h2[hc], "s0",
                                  serialize=(hc == 0))
                    P.wait("vector", "pe", dw_mm[(ck, half, 63)])
                    P.wait("vector", "dve", dsos_done)
                    for b in range(8):
                        dt = half * 8 + b
                        r2_ev[dt] = P.op("vector", lambda e, b=b, dt=dt, ck=ck:
                            e.scalar_tensor_tensor(
                                r2v[:, dt * 512: (dt + 1) * 512],
                                bank(b), dsos_sb[:, dt: dt + 1],
                                y1f16[:, dt * 1024 + ck * 512: dt * 1024 + (ck + 1) * 512],
                                ALU.mult, ALU.add), inc="dve")
                return r2_ev

            ln_prev = [ln1[1]]

            def emit_out(ck, r2_ev):
                tp_done = [0]

                def apply_cb(dt, val):
                    P.wait("tensor", "dve", val)
                    for tt in range(2):
                        P.op("tensor", lambda e, dt=dt, tt=tt: e.transpose(
                            pool[:, tt * 2048 + dt * 128: tt * 2048 + (dt + 1) * 128],
                            r2v[:, dt * 512 + tt * 128: dt * 512 + (tt + 1) * 128],
                            ident_sb[:, :]),
                            inc=("pe" if (dt == 15 and tt == 1) else None))
                res = emit_ln(r2v[:, :],
                              lambda dt: r2v[:, dt * 512: (dt + 1) * 512], f"ln2_{ck}",
                              w_src_dve=r2_ev[15], w_sbf_pe=ln_prev[0]["bc"],
                              w_banks_dve=max(r2_ev[15], ln_prev[0]["last"]),
                              w_banks_act=ln_prev[0]["rows"],
                              f16=False, apply_cb=apply_cb)
                ln_prev[0] = res
                st_t = {}
                ev_t = {}
                for tt in range(4):
                    grp = (tt % 2) * 2048
                    if tt >= 2:
                        P.wait("tensor", "dve", ev_t[tt - 2])
                        for dt in range(16):
                            P.op("tensor", lambda e, dt=dt, tt=tt, grp=grp: e.transpose(
                                pool[:, grp + dt * 128: grp + (dt + 1) * 128],
                                r2v[:, dt * 512 + tt * 128: dt * 512 + (tt + 1) * 128],
                                ident_sb[:, :]),
                                inc=("pe" if dt == 15 else None))
                    tmm = P.cnt["pe"]
                    P.wait("vector", "pe", tmm)
                    P.dwait("vector", "sy0")
                    P.op("vector", lambda e, grp=grp: e.tensor_copy(
                        y2t[:, 0: 1024], pool[:, grp: grp + 1024]))
                    ev_t[tt] = P.op("vector", lambda e, grp=grp: e.tensor_copy(
                        y2t[:, 1024: 2048], pool[:, grp + 1024: grp + 2048]), inc="dve")
                    P.wait("sync", "dve", ev_t[tt])
                    st_t[tt] = P.dma("sync",
                                     d_y[ck * 512 + tt * 128: ck * 512 + (tt + 1) * 128, :],
                                     y2t[:, :], "sy0", serialize=(ck == 0 and tt == 0))
                return st_t[3]

            r2e = emit_down(0, h2_readback=True)
            emit_out(0, r2e)
            P.dwait("tensor", "s0")
            r2e1 = emit_down(1)
            emit_out(1, r2e1)
            P.dwait("sync", "sy0")
            P.dwait("sync", "sy1")

    return nc_finish(nc, P)


def nc_finish(nc, P):
    with nc.Block() as block:
        @block.tensor
        def _(e):
            for fn in P.q["tensor"]:
                fn(e)

        @block.vector
        def _(e):
            for fn in P.q["vector"]:
                fn(e)

        @block.scalar
        def _(e):
            for fn in P.q["scalar"]:
                fn(e)

        @block.gpsimd
        def _(e):
            for fn in P.q["gpsimd"]:
                fn(e)

        @block.sync
        def _(e):
            for fn in P.q["sync"]:
                fn(e)

    return nc


def _prep_shared(Wq, Wkv, Wo, bo, upw, dww, us, ds, os_):
    f32 = np.float32
    wq = np.ascontiguousarray(Wq, dtype=f32)
    wkv = np.ascontiguousarray(Wkv, dtype=f32)
    wo = np.ascontiguousarray(Wo, dtype=f32)
    upt = np.ascontiguousarray(np.asarray(upw, dtype=f32).T)
    upt = np.ascontiguousarray(
        upt.reshape(16, 128, 64, 128).transpose(2, 1, 0, 3).reshape(64, 128, 2048))
    dwt = np.ascontiguousarray(np.asarray(dww, dtype=f32).T)
    boT = np.ascontiguousarray(np.asarray(bo, dtype=f32).reshape(16, 128).T)
    usT = np.ascontiguousarray(np.asarray(us, dtype=f32).reshape(64, 128).T)
    dsT = np.ascontiguousarray(np.asarray(ds, dtype=f32).reshape(16, 128).T)
    osR = np.ascontiguousarray(
        np.broadcast_to(np.asarray(os_, dtype=f32).reshape(1, 1), (128, 1)))
    return dict(wq=wq, wkv=wkv, wo=wo, upt=upt, dwt=dwt, boT=boT, usT=usT,
                dsT=dsT, osR=osR,
                cvec=np.ascontiguousarray(np.broadcast_to(np.array(
                    [[STAB, EPS, -0.5, -1.0 / 2048.0, 1.0 / 2048.0, -1.0, 0.0, 0.0]],
                    dtype=f32), (128, 8))),
                onesd=np.ones((128, 1), dtype=f32),
                ones1d=np.ones((1, 128), dtype=f32),
                identd=np.eye(128, dtype=f32))


def _make_ribbon(j):
    qbs = qb_list(j)
    rib = np.full((128, RIB_W), SENT, dtype=np.float32)
    kl = np.arange(128)[:, None]
    ql = np.arange(128)[None, :]
    for kb in range(32):
        base = int(RIB_OFF[kb])
        jg = kb * 128 + kl
        for s in range(COV[kb]):
            diff = (jg - (qbs[s] * 128 + ql)).astype(np.float32)
            rib[:, base + s * 128: base + (s + 1) * 128] = np.where(diff <= 0, diff, SENT)
    return rib.astype(np.float16)


_CACHE = {}


def kernel(L0_attn_x, L0_attn_Wq, L0_attn_Wkv, L0_attn_Wo, L0_attn_bo,
           L0_ffn_up_weight, L0_ffn_down_weight, L0_ffn_up_scale,
           L0_ffn_down_scale, L0_ffn_output_scale):
    x = np.ascontiguousarray(np.asarray(L0_attn_x), dtype=np.float32)
    shared = _prep_shared(L0_attn_Wq, L0_attn_Wkv, L0_attn_Wo, L0_attn_bo,
                          L0_ffn_up_weight, L0_ffn_down_weight, L0_ffn_up_scale,
                          L0_ffn_down_scale, L0_ffn_output_scale)

    xT4 = {}
    for seq in range(2):
        xt = np.ascontiguousarray(x[seq].T)
        xT4[seq] = np.ascontiguousarray(
            xt.reshape(16, 128, 8, 512).transpose(2, 1, 0, 3))

    in_maps = []
    ribs = {}
    for c in range(NC_):
        seq, j = c // 4, c % 4
        rows = np.concatenate([x[seq][qb * 128: (qb + 1) * 128] for qb in qb_list(j)],
                              axis=0)
        xqT = np.ascontiguousarray(
            np.ascontiguousarray(rows.T).reshape(16, 128, 1024).transpose(1, 0, 2))
        if j not in ribs:
            ribs[j] = _make_ribbon(j)
        m = dict(shared)
        m["xT4"] = xT4[seq]
        m["xqT"] = xqT
        m["rib"] = ribs[j]
        in_maps.append(m)

    if "nc" not in _CACHE:
        _CACHE["nc"] = build_program()
    nc = _CACHE["nc"]

    res = run_bass_kernel_spmd(nc, in_maps, core_ids=list(range(NC_)))

    out = np.empty((2, 4096, 2048), dtype=np.float32)
    for c in range(NC_):
        seq, j = c // 4, c % 4
        yc = np.asarray(res.results[c]["y"])
        for s, qb in enumerate(qb_list(j)):
            out[seq, qb * 128: (qb + 1) * 128, :] = yc[s * 128: (s + 1) * 128, :]
    return out
